# version 12 (final)
# baseline (speedup 1.0000x reference)
import contextlib

import numpy as np

import concourse.bass as bass
import concourse.mybir as mybir
from concourse.bass_utils import run_bass_kernel_spmd

F32 = mybir.dt.float32
BF16 = mybir.dt.bfloat16
U16 = mybir.dt.uint16
U8 = mybir.dt.uint8
AX = mybir.AxisListType
AF = mybir.ActivationFunctionType
ALU = mybir.AluOpType

P = 128
T = 2048
D = 2048
HD = 64
NH = 32
DKV = 512
MLP = 16384
TH = 1024
GELU_C = 0.044715
GELU_S = 0.7978845608028654
BCLAMP = 30.0

PAIRS = [(8 * a + j, 8 * a + j + 4) for a in range(4) for j in range(4)]


def _perm_k():
    return np.concatenate([np.arange(0, D, 2), np.arange(1, D, 2)])


def _perm_n():
    out = []
    for b in range(8):
        out.append(2048 * b + np.arange(0, 2048, 2))
        out.append(2048 * b + np.arange(1, 2048, 2))
    return np.concatenate(out)


def _qcol_order():
    cols = []
    for hl, hh in PAIRS:
        cols.append(np.arange(hl * HD, (hl + 1) * HD))
        cols.append(np.arange(hh * HD, (hh + 1) * HD))
    return np.concatenate(cols)


class Em:

    def __init__(self, nc, which, eng, sems):
        self.nc = nc
        self.which = which
        self.eng = eng
        self.sems = sems
        self.c = {k: 0 for k in sems}

    def proxy(self, name):
        return Em._Proxy(self, name)

    class _Proxy:
        def __init__(self, em, name):
            object.__setattr__(self, "_em", em)
            object.__setattr__(self, "_name", name)

        def __getattr__(self, attr):
            em = self._em
            if em.which == self._name:
                return getattr(em.eng, attr)

            def skip(*a, **kw):
                return None
            return skip

        def wait(self, sem, val=None):
            em = self._em
            v = em.c[sem] if val is None else val
            if em.which == self._name and v > 0:
                em.eng.wait_ge(em.sems[sem], v)

    def inc(self, h, sem, n=1):
        self.c[sem] += n
        if h is not None:
            h.then_inc(self.sems[sem], n)
        return self.c[sem]


def build_program(taps=(), last_phase=4):
    nc = bass.Bass("TRN2", target_bir_lowering=False, detect_race_conditions=False)

    xT = nc.dram_tensor("xT", [D, T], F32, kind="ExternalInput").ap()
    wq = nc.dram_tensor("wq", [D, D], F32, kind="ExternalInput").ap()
    wk = nc.dram_tensor("wk", [D, DKV], F32, kind="ExternalInput").ap()
    wv = nc.dram_tensor("wv", [D, DKV], F32, kind="ExternalInput").ap()
    wo = nc.dram_tensor("wo", [D, D], F32, kind="ExternalInput").ap()
    bout = nc.dram_tensor("bout", [P, 16], F32, kind="ExternalInput").ap()
    gpk = nc.dram_tensor("gpk", [1024 * P, P], U16, kind="ExternalInput").ap()
    upk = nc.dram_tensor("upk", [1024 * P, P], U16, kind="ExternalInput").ap()
    dpk = nc.dram_tensor("dpk", [1024 * P, P], U16, kind="ExternalInput").ap()
    gsc = nc.dram_tensor("gsc", [1024 * P, P], F32, kind="ExternalInput").ap()
    usc = nc.dram_tensor("usc", [1024 * P, P], F32, kind="ExternalInput").ap()
    dsc = nc.dram_tensor("dsc", [1024 * P, P], F32, kind="ExternalInput").ap()
    cosw = nc.dram_tensor("cosw", [T, 32], F32, kind="ExternalInput").ap()
    sinw = nc.dram_tensor("sinw", [T, 32], F32, kind="ExternalInput").ap()
    ident_in = nc.dram_tensor("ident_in", [P, P], F32, kind="ExternalInput").ap()
    trim_in = nc.dram_tensor("trim_in", [P, P], F32, kind="ExternalInput").ap()
    ones_in = nc.dram_tensor("ones_in", [P, P], F32, kind="ExternalInput").ap()

    outT = nc.dram_tensor("outT", [D, T], F32, kind="ExternalOutput").ap()
    xn_hbm = nc.dram_tensor("xn_hbm", [D, T], BF16, kind="Internal").ap()
    sT_hbm = nc.dram_tensor("sT_hbm", [D, T], BF16, kind="Internal").ap()

    tap_t = {}
    for name in taps:
        shp = {"t_xn": [D, T], "t_q": [T, D], "t_kT": [DKV, T],
               "t_ctx": [D, T], "t_s": [D, T], "t_g": [D, T],
               "t_nb": [P, 16], "t_exp": [P, T]}[name]
        tap_t[name] = nc.dram_tensor(name, shp, F32, kind="ExternalOutput").ap()

    with contextlib.ExitStack() as es:
        def sbuf(name, shape, dt):
            return es.enter_context(nc.sbuf_tensor(name, shape, dt))

        ident_bf = sbuf("ident_bf", [P, P], BF16)
        trim_bf = sbuf("trim_bf", [P, P], BF16)
        ones_bf = sbuf("ones_bf", [P, P], BF16)
        cos_sb = sbuf("cos_sb", [P, 512], BF16)
        sin_sb = sbuf("sin_sb", [P, 512], BF16)
        bout_sb = sbuf("bout_sb", [P, 16], F32)
        negB = sbuf("negB", [P, 1], F32)
        nb = sbuf("nb", [P, 8], F32)
        nb_bf = sbuf("nb_bf", [P, 8], BF16)

        tapv_t = sbuf("tapv_t", [P, 512], F32)
        AR1 = sbuf("AR1", [P, 65536], U8)
        AR2 = sbuf("AR2", [P, 65536], U8)
        AR3 = sbuf("AR3", [P, 33280], U8)
        SCR = sbuf("SCR", [P, 34816], U8)
        PS = es.enter_context(nc.psum_tensor("PS", [P, 4096], F32))

        def carve(ar, off, dt, shape, p0=0):
            esz = 2 if dt in (BF16, U16) else 4
            flat = 1
            for s in shape[1:]:
                flat *= s
            v = ar[p0:p0 + shape[0], :].bitcast(dt)[:, off // esz: off // esz + flat]
            if len(shape) == 3:
                v = v.rearrange("p (a b) -> p a b", a=shape[1])
            return v

        def bank(i, w=1):
            return PS[:, 512 * i: 512 * (i + w)]

        xchunk = [carve(AR1, 32768 * i, F32, [P, 16, 512]) for i in range(2)]
        xn = carve(AR2, 0, BF16, [P, 16, T])
        rowAB = carve(SCR, 0, F32, [1, 1024])
        rowST = carve(SCR, 4096, F32, [1, 1024])
        rowBF = carve(SCR, 4096, BF16, [1, 1024])
        ab = carve(SCR, 8192, F32, [P, 1024])
        xbf = carve(SCR, 12288, BF16, [P, 512])
        x2bf = carve(SCR, 13312, BF16, [P, 512])
        lt1 = carve(SCR, 14336, F32, [P, 512])
        tapv = tapv_t[:, :]
        qtok = carve(AR1, 0, BF16, [P, 16, T])
        kT = carve(AR3, 0, BF16, [P, 4, T])
        vaug = carve(AR3, 16384, BF16, [P, 16, 520])
        wchunk = carve(SCR, 0, BF16, [P, 16, 512])
        qsc = carve(SCR, 16384, BF16, [P, 512])
        ropem = [carve(SCR, 17408 + 512 * i, BF16, [P, 256]) for i in range(4)]
        krop = carve(SCR, 19456, BF16, [P, 512])
        qn2m = carve(SCR, 20480, F32, [P, 32])
        kn2m = carve(SCR, 20608, F32, [P, 8])
        qn2c = carve(SCR, 20736, F32, [P, 32])
        kn2c = carve(SCR, 20864, F32, [P, 8])
        q2scr = carve(SCR, 24576, BF16, [P, 2048])
        ctxT = carve(AR2, 0, BF16, [P, 16, T])
        qT = carve(SCR, 0, BF16, [P, T])
        expb = [carve(SCR, 4096 + 2048 * i, BF16, [P, 1024]) for i in range(2)]
        norm32 = carve(SCR, 8192, F32, [65, T])
        normbf = carve(SCR, 16384, BF16, [65, T])
        recip = carve(SCR, 20480, BF16, [64, T])
        oddt = carve(SCR, 24576, BF16, [64, T])
        woch = [carve(SCR, 28672, BF16, [P, 16, P]), carve(AR3, 0, BF16, [P, 16, P])]
        stile = [carve(SCR, 32768 + 1024 * i, BF16, [P, 512]) for i in range(2)]
        xnh = carve(AR1, 0, BF16, [P, 16, TH])
        inter = carve(AR1, 32768, BF16, [P, 16, TH])
        g_acc = carve(AR2, 0, F32, [P, 16, TH])
        wgb = [carve(AR3, 4096 * i, BF16, [P, 16, P]) for i in range(2)]
        wub = [carve(AR3, 8192 + 4096 * i, BF16, [P, 16, P]) for i in range(2)]
        wdb = [carve(AR3, 16384 + 4096 * i, BF16, [P, 16, P]) for i in range(2)]
        gpkb = [carve(AR3, 24576 + 2048 * i, U16, [P, 8, P]) for i in range(2)]
        upkb = [carve(AR3, 28672 + 2048 * i, U16, [P, 8, P]) for i in range(2)]
        dpkb = [carve(SCR, 2048 * i, U16, [P, 8, P]) for i in range(2)]
        dqt1 = carve(SCR, 4096, U16, [P, 1024])
        dqt2 = carve(SCR, 6144, U16, [P, 1024])
        sexp = {m: [carve(SCR, 8192 + 4096 * i + 2048 * j, BF16, [P, 8, P]) for j in range(2)]
                for i, m in enumerate(("g", "u", "d"))}
        glue = {n: carve(SCR, 20480 + 1024 * i, BF16, [P, 512])
                for i, n in enumerate(("gsb", "usb", "g2", "tt", "sg", "gu"))}
        gs_bf = [carve(AR1, 32768 + 2048 * i, BF16, [P, 512]) for i in range(2)]
        x_tl2 = [carve(AR1, 36864 + 4096 * i, F32, [P, 512]) for i in range(2)]
        s_tl = [carve(SCR, 27648 + 2048 * i, BF16, [P, TH]) for i in range(2)]
        x_tl = carve(SCR, 31744, F32, [P, 512])
        tapv3 = tapv_t[:, :]

        sems = {n: es.enter_context(nc.semaphore("c_" + n))
                for n in ("sy", "gp", "pe", "dv", "ac", "ad")}
        block = es.enter_context(nc.Block())

        def program(em):
            V = em.proxy("vector")
            A = em.proxy("scalar")
            Tn = em.proxy("tensor")
            Sy = em.proxy("sync")
            G = em.proxy("gpsimd")

            def tap2d(name, src_view, dst_slice_fn, n0, n1, w):
                if name not in tap_t:
                    return
                for i in range(n0):
                    for j in range(n1):
                        V.wait("pe")
                        V.wait("sy")
                        em.inc(V.tensor_copy(tapv[:, 0:w], src_view(i, j)), "dv")
                        Sy.wait("dv")
                        em.inc(Sy.dma_start(out=dst_slice_fn(i, j), in_=tapv[:, 0:w]), "sy", 16)
                        Sy.wait("sy")

            for dst, src in ((ident_bf, ident_in), (trim_bf, trim_in),
                             (ones_bf, ones_in)):
                em.inc(G.dma_start(out=dst[:], in_=src[:]), "gp", 16)
            em.inc(G.dma_start(out=cos_sb[:].rearrange("p (a b) -> p a b", a=16),
                               in_=cosw.rearrange("(a p) c -> p a c", p=128)), "gp", 16)
            em.inc(G.dma_start(out=sin_sb[:].rearrange("p (a b) -> p a b", a=16),
                               in_=sinw.rearrange("(a p) c -> p a c", p=128)), "gp", 16)
            em.inc(Sy.dma_start(out=bout_sb[:], in_=bout[:]), "sy", 16)
            c_pre_gp = em.c["gp"]

            em.inc(V.memset(vaug[:, :, :].rearrange("p a (g c) -> p a g c", g=8)[:, :, :, 64:65], 1.0), "dv")

            Tn.wait("gp", c_pre_gp)
            for tc in range(4):
                pp = tc % 2
                if tc >= 2:
                    Sy.wait("dv")
                h = Sy.dma_start(out=xchunk[pp][:],
                                 in_=xT.rearrange("(a p) t -> p a t", p=128)[:, :, 512 * tc:512 * (tc + 1)])
                em.inc(h, "sy", 16)
                c_ldx = em.c["sy"]
                for a in range(16):
                    V.wait("sy", c_ldx)
                    V.wait("pe")
                    em.inc(V.tensor_copy(xbf[:], xchunk[pp][:, a, :]), "dv")
                    A.wait("sy", c_ldx)
                    A.wait("pe")
                    em.inc(A.activation(x2bf[:], xchunk[pp][:, a, :], AF.Square), "ac")
                    Tn.wait("dv")
                    Tn.wait("ac")
                    em.inc(Tn.matmul(PS[0:1, 0:512], ones_bf[:, 0:1], xbf[:],
                                     start=(a == 0), stop=(a == 15)), "pe")
                    em.inc(Tn.matmul(PS[0:1, 512:1024], ones_bf[:, 0:1], x2bf[:],
                                     start=(a == 0), stop=(a == 15)), "pe")
                c_sums = em.c["pe"]
                V.wait("pe", c_sums)
                em.inc(V.tensor_scalar(rowST[:, 0:512], PS[0:1, 0:512], 1.0 / 2048.0, None, ALU.mult), "dv")
                em.inc(V.tensor_scalar(rowST[:, 512:1024], PS[0:1, 512:1024], 1.0 / 2048.0, None, ALU.mult), "dv")
                em.inc(V.tensor_tensor(rowAB[:, 0:512], rowST[:, 0:512], rowST[:, 0:512], ALU.mult), "dv")
                em.inc(V.scalar_tensor_tensor(rowAB[:, 0:512], rowAB[:, 0:512], -1.0,
                                              rowST[:, 512:1024], ALU.mult, ALU.add), "dv")
                em.inc(V.tensor_scalar(rowAB[:, 0:512], rowAB[:, 0:512], 1e-5, None, ALU.add), "dv")
                A.wait("dv")
                em.inc(A.activation(rowAB[:, 0:512], rowAB[:, 0:512], AF.Sqrt), "ac")
                V.wait("ac")
                em.inc(V.reciprocal(rowAB[:, 0:512], rowAB[:, 0:512]), "dv")
                em.inc(V.tensor_tensor(rowAB[:, 512:1024], rowST[:, 0:512], rowAB[:, 0:512], ALU.mult), "dv")
                em.inc(V.tensor_scalar(rowAB[:, 512:1024], rowAB[:, 512:1024], -1.0, None, ALU.mult), "dv")
                em.inc(V.tensor_copy(rowBF[:, 0:1024], rowAB[:, 0:1024]), "dv")
                Tn.wait("dv")
                em.inc(Tn.matmul(PS[:, 1024:1536], ones_bf[0:1, :], rowBF[:, 0:512], start=True, stop=True), "pe")
                em.inc(Tn.matmul(PS[:, 1536:2048], ones_bf[0:1, :], rowBF[:, 512:1024], start=True, stop=True), "pe")
                V.wait("pe")
                em.inc(V.tensor_copy(ab[:], PS[:, 1024:2048]), "dv")
                for a in range(16):
                    em.inc(V.tensor_tensor(lt1[:], xchunk[pp][:, a, :], ab[:, 0:512], ALU.mult), "dv")
                    em.inc(V.tensor_tensor(xn[:, a, 512 * tc:512 * (tc + 1)], lt1[:], ab[:, 512:1024], ALU.add), "dv")
                c_xn_tc = em.c["dv"]
                Sy.wait("dv", c_xn_tc)
                h = Sy.dma_start(out=xn_hbm.rearrange("(a p) t -> p a t", p=128)[:, :, 512 * tc:512 * (tc + 1)],
                                 in_=xn[:, :, 512 * tc:512 * (tc + 1)])
                em.inc(h, "sy", 16)
            c_p0_dv = em.c["dv"]
            c_xn_spill = em.c["sy"]

            tap2d("t_xn",
                  lambda a, c: xn[:, a, 512 * c:512 * (c + 1)],
                  lambda a, c: tap_t["t_xn"].rearrange("(a p) t -> p a t", p=128)[:, a, 512 * c:512 * (c + 1)],
                  16, 4, 512)

            if last_phase < 1:
                Sy.wait("sy", em.c["sy"])
                Sy.wait("gp", em.c["gp"])
                return
            em.inc(V.memset(qn2m[:], 0.0), "dv")
            em.inc(V.memset(kn2m[:], 0.0), "dv")
            V.wait("gp", c_pre_gp)

            qcol_w = [wq[:, 512 * i:512 * (i + 1)] for i in range(4)] + [wk, wv]
            c_evac_ac = {0: 0, 1: 0}
            for cc in range(6):
                G.wait("pe")
                G.wait("dv", c_p0_dv)
                h = G.dma_start(out=wchunk[:], in_=qcol_w[cc].rearrange("(a p) n -> p a n", p=128))
                em.inc(h, "gp", 16)
                c_wld = em.c["gp"]
                for tt in range(16):
                    reg = bank(tt % 2)
                    Tn.wait("gp", c_wld)
                    Tn.wait("ac", c_evac_ac[tt % 2])
                    for kt in range(16):
                        em.inc(Tn.matmul(reg, xn[:, kt, 128 * tt:128 * (tt + 1)], wchunk[:, kt, :],
                                         start=(kt == 0), stop=(kt == 15)), "pe")
                    c_mmt = em.c["pe"]
                    A.wait("pe", c_mmt)
                    if cc == 5:
                        em.inc(A.activation(vaug[:, tt, :].rearrange("p (g c) -> p g c", g=8)[:, :, 0:64],
                                            reg.rearrange("p (g c) -> p g c", g=8), AF.Copy), "ac")
                        c_evac_ac[tt % 2] = em.c["ac"]
                        continue
                    A.wait("dv")
                    A.wait("pe")
                    dst = qsc if cc < 4 else krop
                    em.inc(A.activation(dst[:], reg, AF.Copy, scale=(0.125 if cc < 4 else 1.0)), "ac")
                    c_evq = em.c["ac"]
                    c_evac_ac[tt % 2] = c_evq
                    V.wait("ac", c_evq)
                    src3 = dst[:].rearrange("p (g c) -> p g c", g=8)
                    cosv = cos_sb[:, 32 * tt:32 * (tt + 1)].unsqueeze(1).broadcast_to([P, 8, 32])
                    sinv = sin_sb[:, 32 * tt:32 * (tt + 1)].unsqueeze(1).broadcast_to([P, 8, 32])
                    x1 = src3[:, :, 0:32]
                    x2 = src3[:, :, 32:64]
                    m0, m1, m2, m3 = (r[:].rearrange("p (g c) -> p g c", g=8) for r in ropem)
                    em.inc(V.tensor_tensor(m0, x1, cosv, ALU.mult), "dv")
                    em.inc(V.tensor_tensor(m1, x2, sinv, ALU.mult), "dv")
                    em.inc(V.tensor_tensor(m2, x1, sinv, ALU.mult), "dv")
                    em.inc(V.tensor_tensor(m3, x2, cosv, ALU.mult), "dv")
                    if cc < 4:
                        out3 = qtok[:, tt, 512 * cc:512 * (cc + 1)].rearrange("p (g c) -> p g c", g=8)
                    else:
                        out3 = krop[:].rearrange("p (g c) -> p g c", g=8)
                    em.inc(V.tensor_tensor(out3[:, :, 0:32], m0, m1, ALU.subtract), "dv")
                    em.inc(V.tensor_tensor(out3[:, :, 32:64], m2, m3, ALU.add), "dv")
                    c_rope = em.c["dv"]
                    if cc == 4:
                        A.wait("dv", c_rope)
                        em.inc(A.activation(q2scr[:, 0:512], krop[:], AF.Square), "ac")
                        V.wait("ac")
                        for g in range(8):
                            em.inc(V.reduce_sum(kn2c[:, g:g + 1], q2scr[:, 64 * g:64 * (g + 1)], axis=AX.X), "dv")
                        em.inc(V.tensor_tensor(kn2m[:], kn2m[:], kn2c[:], ALU.max), "dv")
                        Tn.wait("dv", c_rope)
                        Tn.wait("dv")
                        tp6 = bank(6).bitcast(BF16)
                        for j in range(4):
                            em.inc(Tn.transpose(tp6[:, 128 * j:128 * (j + 1)],
                                                krop[:, 128 * j:128 * (j + 1)], ident_bf[:]), "pe")
                        c_ktr = em.c["pe"]
                        V.wait("pe", c_ktr)
                        for j in range(4):
                            em.inc(V.tensor_copy(kT[:, j, 128 * tt:128 * (tt + 1)],
                                                 tp6[:, 128 * j:128 * (j + 1)]), "dv")

            A.wait("dv")
            for tt in range(16):
                A.wait("dv")
                em.inc(A.activation(q2scr[:], qtok[:, tt, :], AF.Square), "ac")
                V.wait("ac")
                for b in range(32):
                    em.inc(V.reduce_sum(qn2c[:, b:b + 1], q2scr[:, 64 * b:64 * (b + 1)], axis=AX.X), "dv")
                em.inc(V.tensor_tensor(qn2m[:], qn2m[:], qn2c[:], ALU.max), "dv")
            em.inc(V.reduce_max(nb[:, 0:1], qn2m[:], axis=AX.X), "dv")
            em.inc(V.reduce_max(nb[:, 1:2], kn2m[:], axis=AX.X), "dv")
            nbpad = q2scr[:, 0:128]
            em.inc(V.memset(nbpad, 0.0), "dv")
            em.inc(V.tensor_copy(nbpad[:, 0:2], nb[:, 0:2]), "dv")
            Tn.wait("dv")
            tp7 = bank(7).bitcast(BF16)
            em.inc(Tn.transpose(tp7[:, 0:128], nbpad, ident_bf[:]), "pe")
            V.wait("pe")
            em.inc(V.reduce_max(nb[0:2, 2:3], tp7[0:2, 0:128], axis=AX.X), "dv")
            A.wait("dv")
            em.inc(A.activation(nb[0:2, 3:4], nb[0:2, 2:3], AF.Ln), "ac")
            V.wait("ac")
            em.inc(V.memset(ab[0:2, 0:128], 1.0), "dv")
            em.inc(V.tensor_scalar(ab[0:2, 128:256], ab[0:2, 0:128], nb[0:2, 3:4], None, ALU.mult), "dv")
            em.inc(V.tensor_copy(q2scr[0:2, 0:128], ab[0:2, 128:256]), "dv")
            Tn.wait("dv")
            em.inc(Tn.matmul(PS[:, 3584:3712], ones_bf[0:2, :], q2scr[0:2, 0:128], start=True, stop=True), "pe")
            A.wait("pe")
            em.inc(A.activation(nb[:, 6:7], PS[:, 3584:3585], AF.Exp, scale=0.5), "ac")
            V.wait("ac")
            em.inc(V.tensor_scalar(nb[:, 6:7], nb[:, 6:7], BCLAMP, None, ALU.min), "dv")
            em.inc(V.tensor_scalar(negB[:], nb[:, 6:7], -1.0, None, ALU.mult), "dv")
            if "t_nb" in tap_t:
                em.inc(V.tensor_copy(tapv[:, 0:8], nb[:]), "dv")
                em.inc(V.tensor_copy(tapv[:, 8:9], negB[:]), "dv")
                Sy.wait("dv")
                em.inc(Sy.dma_start(out=tap_t["t_nb"][:, 0:9], in_=tapv[:, 0:9]), "sy", 16)
                Sy.wait("sy")

            tap2d("t_q",
                  lambda tt, c: qtok[:, tt, 512 * c:512 * (c + 1)],
                  lambda tt, c: tap_t["t_q"].rearrange("(a p) t -> p a t", p=128)[:, tt, 512 * c:512 * (c + 1)],
                  16, 4, 512)
            tap2d("t_kT",
                  lambda j, c: kT[:, j, 512 * c:512 * (c + 1)],
                  lambda j, c: tap_t["t_kT"].rearrange("(a p) t -> p a t", p=128)[:, j, 512 * c:512 * (c + 1)],
                  4, 4, 512)

            if last_phase < 2:
                Sy.wait("sy", em.c["sy"])
                Sy.wait("gp", em.c["gp"])
                return
            c_exp_hist = {0: 0, 1: 0}
            c_bank03_dv = 0
            for ip, (hl, hh) in enumerate(PAIRS):
                a_tile = ip // 4
                tq01 = PS[:, 0:1024].bitcast(BF16)
                Tn.wait("ac", max(c_exp_hist[0], c_exp_hist[1]))
                Tn.wait("dv", c_bank03_dv if ip else 0)
                for tt in range(16):
                    em.inc(Tn.transpose(tq01[:, 128 * tt:128 * (tt + 1)],
                                        qtok[:, tt, 128 * ip:128 * (ip + 1)], ident_bf[:]), "pe")
                c_qtr = em.c["pe"]
                V.wait("pe", c_qtr)
                em.inc(V.tensor_copy(qT[:], tq01[:]), "dv")
                c_qT = em.c["dv"]
                for hi, h_ in enumerate((hl, hh)):
                    base = 64 * hi
                    g = h_ // 4
                    steps = []
                    for tkt in range(16):
                        for half in range(2):
                            lo = 128 * tkt + 1024 * half
                            hiw = min(128 * tkt + 1024 * (half + 1), T)
                            if lo < hiw:
                                steps.append((tkt, half, lo, hiw))
                    sdep = [None] * len(steps)
                    for i in range(len(steps) + 1):
                        if i < len(steps):
                            tkt, half, lo, hiw = steps[i]
                            w = hiw - lo
                            ppq = i % 2
                            Tn.wait("ac", c_exp_hist[ppq])
                            if i < 2:
                                Tn.wait("dv", max(c_qT, c_bank03_dv))
                            for j in range((w + 511) // 512):
                                wj = min(512, w - 512 * j)
                                em.inc(Tn.matmul(PS[:, 1024 * ppq + 512 * j:1024 * ppq + 512 * j + wj],
                                                 kT[base:base + 64, a_tile, 128 * tkt:128 * (tkt + 1)],
                                                 qT[base:base + 64, lo + 512 * j:lo + 512 * j + wj],
                                                 start=True, stop=True), "pe")
                            c_smm = em.c["pe"]
                            A.wait("pe", c_smm)
                            em.inc(A.activation(expb[ppq][:, 0:w], PS[:, 1024 * ppq:1024 * ppq + w],
                                                AF.Exp, bias=negB[:]), "ac")
                            c_exp_hist[ppq] = em.c["ac"]
                            if "t_exp" in tap_t and ip == 0 and hi == 0 and i == 0:
                                V.wait("ac")
                                em.inc(V.tensor_copy(tapv[:, 0:512], expb[ppq][:, 0:512]), "dv")
                                Sy.wait("dv")
                                em.inc(Sy.dma_start(out=tap_t["t_exp"][:, 0:512], in_=tapv[:, 0:512]), "sy", 16)
                                Sy.wait("sy")
                            if half == 0:
                                V.wait("ac", c_exp_hist[ppq])
                                V.wait("pe", c_smm)
                                em.inc(V.tensor_tensor(expb[ppq][:, 0:128], expb[ppq][:, 0:128],
                                                       trim_bf[:], ALU.mult), "dv")
                                sdep[i] = ("dv", em.c["dv"])
                            else:
                                sdep[i] = ("ac", c_exp_hist[ppq])
                        if i >= 1:
                            tkt, half, lo, hiw = steps[i - 1]
                            ppq = (i - 1) % 2
                            sem, cnt = sdep[i - 1]
                            Tn.wait(sem, cnt)
                            pos = lo
                            while pos < hiw:
                                cbank = pos // 512
                                cw = min(512 * (cbank + 1), hiw) - pos
                                em.inc(Tn.matmul(PS[0:65, 2048 + pos:2048 + pos + cw],
                                                 vaug[:, tkt, 65 * g:65 * g + 65],
                                                 expb[ppq][:, pos - lo:pos - lo + cw],
                                                 start=(tkt == 0), stop=(tkt == 4 * cbank + 3)), "pe")
                                pos += cw
                    c_av = em.c["pe"]
                    A.wait("pe", c_av)
                    em.inc(A.activation(norm32[64:65, :], PS[64:65, 2048:4096], AF.Ln), "ac")
                    em.inc(A.activation(normbf[64:65, :], norm32[64:65, :], AF.Exp, scale=-1.0), "ac")
                    c_rc = em.c["ac"]
                    Tn.wait("ac", c_rc)
                    for j in range(4):
                        em.inc(Tn.matmul(PS[0:64, 512 * j:512 * (j + 1)], ones_bf[64:65, 0:64],
                                         normbf[64:65, 512 * j:512 * (j + 1)], start=True, stop=True), "pe")
                    V.wait("pe")
                    em.inc(V.tensor_copy(recip[:], PS[0:64, 0:2048]), "dv")
                    c_bank03_dv = em.c["dv"]
                    if h_ % 2 == 0:
                        em.inc(V.tensor_tensor(ctxT[0:64, h_ // 2, :], PS[0:64, 2048:4096],
                                               recip[:], ALU.mult), "dv")
                    else:
                        V.wait("gp")
                        em.inc(V.tensor_tensor(oddt[:], PS[0:64, 2048:4096], recip[:], ALU.mult), "dv")
                        c_odd = em.c["dv"]
                        G.wait("dv", c_odd)
                        em.inc(G.dma_start(out=ctxT[64:128, h_ // 2, :], in_=oddt[:]), "gp", 16)
                    Tn.wait("dv")
            c_ctx_gp = em.c["gp"]

            tap2d("t_ctx",
                  lambda a, c: ctxT[:, a, 512 * c:512 * (c + 1)],
                  lambda a, c: tap_t["t_ctx"].rearrange("(a p) t -> p a t", p=128)[:, a, 512 * c:512 * (c + 1)],
                  16, 4, 512)

            c_sev_hist = {0: 0, 1: 0}
            c_wo_mms = {0: 0, 1: 0}
            c_wo_ld = {}
            for ph in range(17):
                if ph < 16:
                    ht = ph
                    G.wait("pe", c_wo_mms[ht % 2])
                    if ht == 0:
                        G.wait("pe")
                    em.inc(G.dma_start(out=woch[ht % 2][:],
                                       in_=wo[:, 128 * ht:128 * (ht + 1)].rearrange("(a p) n -> p a n", p=128)), "gp", 16)
                    c_wo_ld[ht] = em.c["gp"]
                if ph >= 1:
                    ht = ph - 1
                    for tc in range(4):
                        reg = bank(tc % 2)
                        Tn.wait("gp", c_wo_ld[ht])
                        if ht == 0 and tc == 0:
                            Tn.wait("gp", c_ctx_gp)
                            Tn.wait("dv")
                        Tn.wait("dv", c_sev_hist[tc % 2])
                        for kt in range(16):
                            em.inc(Tn.matmul(reg, woch[ht % 2][:, kt, :], ctxT[:, kt, 512 * tc:512 * (tc + 1)],
                                             start=(kt == 0), stop=(kt == 15)), "pe")
                        c_so = em.c["pe"]
                        c_wo_mms[ht % 2] = c_so
                        V.wait("pe", c_so)
                        V.wait("sy")
                        em.inc(V.tensor_scalar(stile[tc % 2][:], reg, bout_sb[:, ht:ht + 1], None, ALU.add), "dv")
                        c_sev = em.c["dv"]
                        c_sev_hist[tc % 2] = c_sev
                        Sy.wait("dv", c_sev)
                        em.inc(Sy.dma_start(out=sT_hbm[128 * ht:128 * (ht + 1), 512 * tc:512 * (tc + 1)],
                                            in_=stile[tc % 2][:]), "sy", 16)
            c_s_spill = em.c["sy"]

            if last_phase < 3:
                Sy.wait("sy", em.c["sy"])
                Sy.wait("gp", em.c["gp"])
                return
            P126 = 2.0 ** 126

            def dequant(pkb_flat, wtile, sxbuf):
                sx_flat = sxbuf[:].rearrange("p a n -> p (a n)")
                em.inc(V.tensor_scalar(dqt1[:], pkb_flat, 12, 0x8000, ALU.logical_shift_left, ALU.bitwise_and), "dv")
                em.inc(V.tensor_scalar(dqt2[:], pkb_flat, 6, 0x1C0, ALU.logical_shift_left, ALU.bitwise_and), "dv")
                em.inc(V.tensor_tensor(dqt1[:], dqt1[:], dqt2[:], ALU.bitwise_or), "dv")
                em.inc(V.scalar_tensor_tensor(wtile[:, 0:1024], dqt1[:].bitcast(BF16), P126, sx_flat, ALU.mult, ALU.mult), "dv")
                em.inc(V.tensor_scalar(dqt1[:], pkb_flat, 8, 0x8000, ALU.logical_shift_left, ALU.bitwise_and), "dv")
                em.inc(V.tensor_scalar(dqt2[:], pkb_flat, 2, 0x1C0, ALU.logical_shift_left, ALU.bitwise_and), "dv")
                em.inc(V.tensor_tensor(dqt1[:], dqt1[:], dqt2[:], ALU.bitwise_or), "dv")
                em.inc(V.scalar_tensor_tensor(wtile[:, 1024:2048], dqt1[:].bitcast(BF16), P126, sx_flat, ALU.mult, ALU.mult), "dv")
                return em.c["dv"]

            c_dq_hist = {}
            c_mms_hist = {}
            c_glue_dv = {0: 0, 1: 0}
            c_sq_ac = {0: 0, 1: 0}
            c_dqd_hist = {}
            c_dmm_hist = {}
            c_gacc_hist = {}
            def gate_load_dq(nsk2, ns2, sub):
                ksub = nsk2 * 16 + sub
                gsub = 16 * ns2 + sub
                pp = gsub % 2
                war = c_dq_hist.get(ksub - 2, 0)
                Sy.wait("dv", war)
                G.wait("dv", war)
                em.inc(Sy.dma_start(out=gpkb[pp][:],
                                    in_=gpk[1024 * gsub:1024 * (gsub + 1), :].rearrange("(a p) n -> p a n", p=128)), "sy", 16)
                em.inc(Sy.dma_start(out=upkb[pp][:],
                                    in_=upk[1024 * gsub:1024 * (gsub + 1), :].rearrange("(a p) n -> p a n", p=128)), "sy", 16)
                c_pk = em.c["sy"]
                em.inc(G.dma_start(out=sexp["g"][pp][:],
                                   in_=gsc[1024 * gsub:1024 * (gsub + 1), :].rearrange("(a p) n -> p a n", p=128)), "gp", 16)
                em.inc(G.dma_start(out=sexp["u"][pp][:],
                                   in_=usc[1024 * gsub:1024 * (gsub + 1), :].rearrange("(a p) n -> p a n", p=128)), "gp", 16)
                c_gpk = em.c["gp"]
                V.wait("pe", c_mms_hist.get(ksub - 2, 0))
                V.wait("sy", c_pk)
                V.wait("gp", c_gpk)
                dequant(gpkb[pp][:].rearrange("p a n -> p (a n)"),
                        wgb[pp][:].rearrange("p a n -> p (a n)"), sexp["g"][pp])
                c_dq_hist[ksub] = dequant(upkb[pp][:].rearrange("p a n -> p (a n)"),
                                          wub[pp][:].rearrange("p a n -> p (a n)"), sexp["u"][pp])

            def down_load_dq(nsk2, ns2, hc):
                khc = nsk2 * 16 + hc
                ppd = hc % 2
                war = c_dqd_hist.get(khc - 2, 0)
                Sy.wait("dv", war)
                G.wait("dv", war)
                em.inc(Sy.dma_start(out=dpkb[ppd][:],
                                    in_=dpk[8192 * hc + 1024 * ns2:8192 * hc + 1024 * (ns2 + 1), :].rearrange("(a p) n -> p a n", p=128)), "sy", 16)
                c_dl = em.c["sy"]
                em.inc(G.dma_start(out=sexp["d"][ppd][:],
                                   in_=dsc[8192 * hc + 1024 * ns2:8192 * hc + 1024 * (ns2 + 1), :].rearrange("(a p) n -> p a n", p=128)), "gp", 16)
                c_gl = em.c["gp"]
                V.wait("pe", c_dmm_hist.get(khc - 2, 0))
                V.wait("sy", c_dl)
                V.wait("gp", c_gl)
                c_dqd_hist[khc] = dequant(dpkb[ppd][:].rearrange("p a n -> p (a n)"),
                                          wdb[ppd][:].rearrange("p a n -> p (a n)"), sexp["d"][ppd])

            for th in range(2):
                t0 = TH * th
                Sy.wait("sy", c_xn_spill)
                if th == 1:
                    Sy.wait("pe")
                em.inc(Sy.dma_start(out=xnh[:],
                                    in_=xn_hbm.rearrange("(a p) t -> p a t", p=128)[:, :, t0:t0 + TH]), "sy", 16)
                c_xnh = em.c["sy"]
                for ns in range(8):
                    nsk = th * 8 + ns
                    first_block = (th == 0 and ns == 0)
                    for ph in range(17):
                        if ph < 16 and (ph >= 2 or first_block):
                            gate_load_dq(nsk, ns, ph)
                        if ph == 15:
                            down_load_dq(nsk, ns, 0)
                        if ph == 16:
                            down_load_dq(nsk, ns, 1)
                        if ph >= 1:
                            sub = ph - 1
                            ksub = nsk * 16 + sub
                            pp = (16 * ns + sub) % 2
                            for qt in range(2):
                                q0 = 512 * qt
                                gb = bank(qt)
                                ub = bank(2 + qt)
                                Tn.wait("dv", max(c_dq_hist[ksub], c_glue_dv[qt]))
                                Tn.wait("sy", c_xnh)
                                Tn.wait("ac", c_sq_ac[qt])
                                for kt in range(16):
                                    em.inc(Tn.matmul(gb, wgb[pp][:, kt, :], xnh[:, kt, q0:q0 + 512],
                                                     start=(kt == 0), stop=(kt == 15)), "pe")
                                for kt in range(16):
                                    em.inc(Tn.matmul(ub, wub[pp][:, kt, :], xnh[:, kt, q0:q0 + 512],
                                                     start=(kt == 0), stop=(kt == 15)), "pe")
                                c_gu = em.c["pe"]
                                A.wait("pe", c_gu)
                                A.wait("dv", max(c_glue_dv[0], c_glue_dv[1]))
                                em.inc(A.activation(glue["gsb"][:], gb, AF.Copy), "ac")
                                em.inc(A.activation(glue["usb"][:], ub, AF.Copy), "ac")
                                em.inc(A.activation(glue["g2"][:], gb, AF.Square), "ac")
                                c_sq_ac[qt] = em.c["ac"]
                                V.wait("ac")
                                em.inc(V.tensor_scalar(glue["g2"][:], glue["g2"][:], GELU_C, 1.0,
                                                       ALU.mult, ALU.add), "dv")
                                em.inc(V.tensor_tensor(glue["tt"][:], glue["g2"][:], glue["gsb"][:], ALU.mult), "dv")
                                em.inc(V.tensor_tensor(glue["gu"][:], glue["gsb"][:], glue["usb"][:], ALU.mult), "dv")
                                A.wait("dv")
                                em.inc(A.activation(glue["sg"][:], glue["tt"][:], AF.Sigmoid, scale=2.0 * GELU_S), "ac")
                                V.wait("ac")
                                em.inc(V.tensor_tensor(inter[:, sub, q0:q0 + 512], glue["sg"][:], glue["gu"][:], ALU.mult), "dv")
                                c_glue_dv[qt] = em.c["dv"]
                            c_mms_hist[ksub] = c_gu
                    for ph in range(17):
                        if 2 <= ph < 16:
                            down_load_dq(nsk, ns, ph)
                        if ph == 15 and not (th == 1 and ns == 7):
                            ns2, nsk2 = (ns + 1, nsk + 1) if ns < 7 else (0, nsk + 1)
                            gate_load_dq(nsk2, ns2, 0)
                        if ph == 16 and not (th == 1 and ns == 7):
                            ns2, nsk2 = (ns + 1, nsk + 1) if ns < 7 else (0, nsk + 1)
                            gate_load_dq(nsk2, ns2, 1)
                        if ph >= 1:
                            hc = ph - 1
                            khc = nsk * 16 + hc
                            ppd = hc % 2
                            for qt2 in range(2):
                                q0 = 512 * qt2
                                db = bank(4 + 2 * ppd + qt2)
                                Tn.wait("dv", max(c_dqd_hist[khc], c_gacc_hist.get((khc - 2, qt2), 0)))
                                Tn.wait("sy", c_xnh)
                                for kt in range(16):
                                    em.inc(Tn.matmul(db, wdb[ppd][:, kt, :], inter[:, kt, q0:q0 + 512],
                                                     start=(kt == 0), stop=(kt == 15)), "pe")
                                c_dmm = em.c["pe"]
                                V.wait("pe", c_dmm)
                                if ns == 0:
                                    em.inc(V.tensor_copy(g_acc[:, hc, q0:q0 + 512], db), "dv")
                                else:
                                    em.inc(V.tensor_tensor(g_acc[:, hc, q0:q0 + 512],
                                                           g_acc[:, hc, q0:q0 + 512], db, ALU.add), "dv")
                                c_gacc_hist[(khc, qt2)] = em.c["dv"]
                            c_dmm_hist[khc] = c_dmm
                c_gacc = em.c["dv"]
                c_comb_hist = {}
                c_gsb_hist = {}
                c_xld_hist = {}
                c_sld_hist = {}
                for i in range(32):
                    hc, hh = i // 2, i % 2
                    pb = i % 2
                    if hh == 0:
                        A.wait("sy", c_s_spill)
                        A.wait("dv", c_comb_hist.get((hc - 2, 1), 0))
                        em.inc(A.dma_start(out=s_tl[hc % 2][:],
                                           in_=sT_hbm[128 * hc:128 * (hc + 1), t0:t0 + TH]), "ad", 16)
                        c_sld_hist[hc] = em.c["ad"]
                    tq0 = t0 + 512 * hh
                    xin = xT.rearrange("(h r) t -> r h t", h=2)[64 * hc:64 * (hc + 1), :, tq0:tq0 + 512]
                    if i < 2:
                        A.wait("pe")
                    A.wait("dv", c_comb_hist.get((hc - 1, hh), 0))
                    em.inc(A.dma_start(out=x_tl2[pb][:], in_=xin), "ad", 16)
                    c_xld_hist[i] = em.c["ad"]
                    A.wait("dv", max(c_gacc, c_gsb_hist.get(i - 2, 0)))
                    em.inc(A.activation(gs_bf[pb][:], g_acc[:, hc, 512 * hh:512 * (hh + 1)], AF.Silu), "ac")
                    c_gs = em.c["ac"]
                    V.wait("ac", c_gs)
                    V.wait("ad", max(c_xld_hist[i], c_sld_hist[hc]))
                    em.inc(V.tensor_tensor(gs_bf[pb][:], gs_bf[pb][:], s_tl[hc % 2][:, 512 * hh:512 * (hh + 1)], ALU.mult), "dv")
                    c_gsb_hist[i] = em.c["dv"]
                    em.inc(V.tensor_tensor(x_tl2[pb][:], x_tl2[pb][:], gs_bf[pb][:], ALU.add), "dv")
                    c_comb_hist[(hc, hh)] = em.c["dv"]
                    A.wait("dv", c_comb_hist[(hc, hh)])
                    em.inc(A.dma_start(out=outT[128 * hc:128 * (hc + 1), tq0:tq0 + 512], in_=x_tl2[pb][:]), "ad", 16)

                if "t_g" in tap_t:
                    for hc in range(16):
                        for c2 in range(2):
                            V.wait("sy")
                            em.inc(V.tensor_copy(tapv3[:], g_acc[:, hc, 512 * c2:512 * (c2 + 1)]), "dv")
                            Sy.wait("dv")
                            em.inc(Sy.dma_start(out=tap_t["t_g"][128 * hc:128 * (hc + 1),
                                                               t0 + 512 * c2:t0 + 512 * (c2 + 1)],
                                                in_=tapv3[:]), "sy", 16)
                            Sy.wait("sy")

            Sy.wait("sy", em.c["sy"])
            Sy.wait("gp", em.c["gp"])
            A.wait("ad", em.c["ad"])

        def mk(which):
            def cb(eng):
                program(Em(nc, which, eng, sems))
            return cb

        block.sync(mk("sync"))
        block.gpsimd(mk("gpsimd"))
        block.tensor(mk("tensor"))
        block.vector(mk("vector"))
        block.scalar(mk("scalar"))

    return nc


def host_inputs(Wq, Wk, Wv, Wo, b_out, gWp, gSc, uWp, uSc, dWp, dSc):
    pk = _perm_k()
    pn = _perm_n()
    qc = _qcol_order()

    def pack_blocks(Wp, perm_rows):
        Wt = np.ascontiguousarray(Wp.astype(np.uint16)[perm_rows, :].T)
        K2, N = Wt.shape
        nb = N // P
        return np.ascontiguousarray(Wt.reshape(K2, nb, P).transpose(1, 0, 2)).reshape(nb * K2, P)

    def pack_scales(Sc, perm_rows):
        St = np.ascontiguousarray(Sc[perm_rows, :].T.astype(np.float32))
        B, N = St.shape
        nb = N // P
        blocks = np.ascontiguousarray(St.reshape(B, nb, P).transpose(1, 0, 2))
        return np.ascontiguousarray(np.repeat(blocks, 16, axis=1)).reshape(nb * B * 16, P)

    inv = (1.0 / (10000.0 ** (np.arange(0, HD, 2, dtype=np.float32) / np.float32(HD)))).astype(np.float32)
    t = np.arange(T, dtype=np.float32)
    fr = np.outer(t, inv).astype(np.float32)

    return {
        "wq": np.ascontiguousarray(Wq[pk][:, qc]).astype(np.float32),
        "wk": np.ascontiguousarray(Wk[pk]).astype(np.float32),
        "wv": np.ascontiguousarray(Wv[pk]).astype(np.float32),
        "wo": np.ascontiguousarray(Wo).astype(np.float32),
        "bout": np.ascontiguousarray(np.asarray(b_out).reshape(16, P).T).astype(np.float32),
        "gpk": pack_blocks(gWp, pn),
        "upk": pack_blocks(uWp, pn),
        "dpk": pack_blocks(dWp, np.arange(D)),
        "gsc": pack_scales(gSc, pn),
        "usc": pack_scales(uSc, pn),
        "dsc": pack_scales(dSc, np.arange(D)),
        "cosw": np.cos(fr).astype(np.float32),
        "sinw": np.sin(fr).astype(np.float32),
        "ident_in": np.eye(P, dtype=np.float32),
        "trim_in": np.triu(np.ones((P, P), dtype=np.float32)),
        "ones_in": np.ones((P, P), dtype=np.float32),
    }


def kernel(L0_attn_x, L0_attn_Wq, L0_attn_Wk, L0_attn_Wv, L0_attn_Wo, L0_attn_b_out,
           L0_ffn_gate_W_packed, L0_ffn_gate_scales, L0_ffn_up_W_packed, L0_ffn_up_scales,
           L0_ffn_down_W_packed, L0_ffn_down_scales):
    x = np.asarray(L0_attn_x)
    shared = host_inputs(np.asarray(L0_attn_Wq), np.asarray(L0_attn_Wk),
                         np.asarray(L0_attn_Wv), np.asarray(L0_attn_Wo),
                         np.asarray(L0_attn_b_out),
                         np.asarray(L0_ffn_gate_W_packed), np.asarray(L0_ffn_gate_scales),
                         np.asarray(L0_ffn_up_W_packed), np.asarray(L0_ffn_up_scales),
                         np.asarray(L0_ffn_down_W_packed), np.asarray(L0_ffn_down_scales))
    pk = _perm_k()
    nc = build_program()
    in_maps = []
    for b in range(8):
        m = dict(shared)
        m["xT"] = np.ascontiguousarray(x[b].T[pk, :]).astype(np.float32)
        in_maps.append(m)
    res = run_bass_kernel_spmd(nc, in_maps, core_ids=list(range(8)))
    out = np.empty((8, T, D), dtype=np.float32)
    for b in range(8):
        out[b] = res.results[b]["outT"].T
    return out


# version 13
# speedup vs baseline: 1.0060x; 1.0060x over previous
import contextlib

import numpy as np

import concourse.bass as bass
import concourse.mybir as mybir
from concourse.bass_utils import run_bass_kernel_spmd

F32 = mybir.dt.float32
BF16 = mybir.dt.bfloat16
U16 = mybir.dt.uint16
U8 = mybir.dt.uint8
AX = mybir.AxisListType
AF = mybir.ActivationFunctionType
ALU = mybir.AluOpType

P = 128
T = 2048
D = 2048
HD = 64
NH = 32
DKV = 512
MLP = 16384
TH = 1024
GELU_C = 0.044715
GELU_S = 0.7978845608028654
BCLAMP = 30.0

PAIRS = [(8 * a + j, 8 * a + j + 4) for a in range(4) for j in range(4)]


def _perm_k():
    return np.concatenate([np.arange(0, D, 2), np.arange(1, D, 2)])


def _perm_n():
    out = []
    for b in range(8):
        out.append(2048 * b + np.arange(0, 2048, 2))
        out.append(2048 * b + np.arange(1, 2048, 2))
    return np.concatenate(out)


def _qcol_order():
    cols = []
    for hl, hh in PAIRS:
        cols.append(np.arange(hl * HD, (hl + 1) * HD))
        cols.append(np.arange(hh * HD, (hh + 1) * HD))
    return np.concatenate(cols)


class Em:

    def __init__(self, nc, which, eng, sems):
        self.nc = nc
        self.which = which
        self.eng = eng
        self.sems = sems
        self.c = {k: 0 for k in sems}

    def proxy(self, name):
        return Em._Proxy(self, name)

    class _Proxy:
        def __init__(self, em, name):
            object.__setattr__(self, "_em", em)
            object.__setattr__(self, "_name", name)

        def __getattr__(self, attr):
            em = self._em
            if em.which == self._name:
                return getattr(em.eng, attr)

            def skip(*a, **kw):
                return None
            return skip

        def wait(self, sem, val=None):
            em = self._em
            v = em.c[sem] if val is None else val
            if em.which == self._name and v > 0:
                em.eng.wait_ge(em.sems[sem], v)

    def inc(self, h, sem, n=1):
        self.c[sem] += n
        if h is not None:
            h.then_inc(self.sems[sem], n)
        return self.c[sem]


def build_program(taps=(), last_phase=4):
    nc = bass.Bass("TRN2", target_bir_lowering=False, detect_race_conditions=False)

    xT = nc.dram_tensor("xT", [D, T], F32, kind="ExternalInput").ap()
    wq = nc.dram_tensor("wq", [D, D], F32, kind="ExternalInput").ap()
    wk = nc.dram_tensor("wk", [D, DKV], F32, kind="ExternalInput").ap()
    wv = nc.dram_tensor("wv", [D, DKV], F32, kind="ExternalInput").ap()
    wo = nc.dram_tensor("wo", [D, D], F32, kind="ExternalInput").ap()
    bout = nc.dram_tensor("bout", [P, 16], F32, kind="ExternalInput").ap()
    gpk = nc.dram_tensor("gpk", [1024 * P, P], U16, kind="ExternalInput").ap()
    upk = nc.dram_tensor("upk", [1024 * P, P], U16, kind="ExternalInput").ap()
    dpk = nc.dram_tensor("dpk", [1024 * P, P], U16, kind="ExternalInput").ap()
    gsc = nc.dram_tensor("gsc", [1024 * P, P], F32, kind="ExternalInput").ap()
    usc = nc.dram_tensor("usc", [1024 * P, P], F32, kind="ExternalInput").ap()
    dsc = nc.dram_tensor("dsc", [1024 * P, P], F32, kind="ExternalInput").ap()
    cosw = nc.dram_tensor("cosw", [T, 32], F32, kind="ExternalInput").ap()
    sinw = nc.dram_tensor("sinw", [T, 32], F32, kind="ExternalInput").ap()
    ident_in = nc.dram_tensor("ident_in", [P, P], F32, kind="ExternalInput").ap()
    trim_in = nc.dram_tensor("trim_in", [P, P], F32, kind="ExternalInput").ap()
    ones_in = nc.dram_tensor("ones_in", [P, P], F32, kind="ExternalInput").ap()

    outT = nc.dram_tensor("outT", [D, T], F32, kind="ExternalOutput").ap()
    xn_hbm = nc.dram_tensor("xn_hbm", [D, T], BF16, kind="Internal").ap()
    sT_hbm = nc.dram_tensor("sT_hbm", [D, T], BF16, kind="Internal").ap()

    tap_t = {}
    for name in taps:
        shp = {"t_xn": [D, T], "t_q": [T, D], "t_kT": [DKV, T],
               "t_ctx": [D, T], "t_s": [D, T], "t_g": [D, T],
               "t_nb": [P, 16], "t_exp": [P, T]}[name]
        tap_t[name] = nc.dram_tensor(name, shp, F32, kind="ExternalOutput").ap()

    with contextlib.ExitStack() as es:
        def sbuf(name, shape, dt):
            return es.enter_context(nc.sbuf_tensor(name, shape, dt))

        ident_bf = sbuf("ident_bf", [P, P], BF16)
        trim_bf = sbuf("trim_bf", [P, P], BF16)
        ones_bf = sbuf("ones_bf", [P, P], BF16)
        cos_sb = sbuf("cos_sb", [P, 512], BF16)
        sin_sb = sbuf("sin_sb", [P, 512], BF16)
        bout_sb = sbuf("bout_sb", [P, 16], F32)
        negB = sbuf("negB", [P, 1], F32)
        nb = sbuf("nb", [P, 8], F32)
        nb_bf = sbuf("nb_bf", [P, 8], BF16)

        tapv_t = sbuf("tapv_t", [P, 512], F32)
        AR1 = sbuf("AR1", [P, 65536], U8)
        AR2 = sbuf("AR2", [P, 65536], U8)
        AR3 = sbuf("AR3", [P, 33280], U8)
        SCR = sbuf("SCR", [P, 34816], U8)
        PS = es.enter_context(nc.psum_tensor("PS", [P, 4096], F32))

        def carve(ar, off, dt, shape, p0=0):
            esz = 2 if dt in (BF16, U16) else 4
            flat = 1
            for s in shape[1:]:
                flat *= s
            v = ar[p0:p0 + shape[0], :].bitcast(dt)[:, off // esz: off // esz + flat]
            if len(shape) == 3:
                v = v.rearrange("p (a b) -> p a b", a=shape[1])
            return v

        def bank(i, w=1):
            return PS[:, 512 * i: 512 * (i + w)]

        xchunk = [carve(AR1, 32768 * i, F32, [P, 16, 512]) for i in range(2)]
        xn = carve(AR2, 0, BF16, [P, 16, T])
        rowAB = carve(SCR, 0, F32, [1, 1024])
        rowST = carve(SCR, 4096, F32, [1, 1024])
        rowBF = carve(SCR, 4096, BF16, [1, 1024])
        ab = carve(SCR, 8192, F32, [P, 1024])
        xbf = [carve(SCR, 12288 + 1024 * i, BF16, [P, 512]) for i in range(2)]
        x2bf = [carve(SCR, 14336 + 1024 * i, BF16, [P, 512]) for i in range(2)]
        lt1 = carve(SCR, 16384, F32, [P, 512])
        tapv = tapv_t[:, :]
        qtok = carve(AR1, 0, BF16, [P, 16, T])
        kT = carve(AR3, 0, BF16, [P, 4, T])
        vaug = carve(AR3, 16384, BF16, [P, 16, 520])
        wchunk = carve(SCR, 0, BF16, [P, 16, 512])
        qsc = carve(SCR, 16384, BF16, [P, 512])
        ropem = [carve(SCR, 17408 + 512 * i, BF16, [P, 256]) for i in range(4)]
        krop = carve(SCR, 19456, BF16, [P, 512])
        qn2m = carve(SCR, 20480, F32, [P, 32])
        kn2m = carve(SCR, 20608, F32, [P, 8])
        qn2c = carve(SCR, 20736, F32, [P, 32])
        kn2c = carve(SCR, 20864, F32, [P, 8])
        q2scr = carve(SCR, 24576, BF16, [P, 2048])
        ctxT = carve(AR2, 0, BF16, [P, 16, T])
        qT = carve(SCR, 0, BF16, [P, T])
        expb = [carve(SCR, 4096 + 2048 * i, BF16, [P, 1024]) for i in range(2)]
        norm32 = carve(SCR, 8192, F32, [65, T])
        normbf = carve(SCR, 16384, BF16, [65, T])
        recip = carve(SCR, 20480, BF16, [64, T])
        oddt = carve(SCR, 24576, BF16, [64, T])
        woch = [carve(SCR, 28672, BF16, [P, 16, P]), carve(AR3, 0, BF16, [P, 16, P])]
        stile = [carve(SCR, 32768 + 1024 * i, BF16, [P, 512]) for i in range(2)]
        xnh = carve(AR1, 0, BF16, [P, 16, TH])
        inter = carve(AR1, 32768, BF16, [P, 16, TH])
        g_acc = carve(AR2, 0, F32, [P, 16, TH])
        wgb = [carve(AR3, 4096 * i, BF16, [P, 16, P]) for i in range(2)]
        wub = [carve(AR3, 8192 + 4096 * i, BF16, [P, 16, P]) for i in range(2)]
        wdb = [carve(AR3, 16384 + 4096 * i, BF16, [P, 16, P]) for i in range(2)]
        gpkb = [carve(AR3, 24576 + 2048 * i, U16, [P, 8, P]) for i in range(2)]
        upkb = [carve(AR3, 28672 + 2048 * i, U16, [P, 8, P]) for i in range(2)]
        dpkb = [carve(SCR, 2048 * i, U16, [P, 8, P]) for i in range(2)]
        dqt1 = carve(SCR, 4096, U16, [P, 1024])
        dqt2 = carve(SCR, 6144, U16, [P, 1024])
        sexp = {m: [carve(SCR, 8192 + 4096 * i + 2048 * j, BF16, [P, 8, P]) for j in range(2)]
                for i, m in enumerate(("g", "u", "d"))}
        glue = {n: carve(SCR, 20480 + 1024 * i, BF16, [P, 512])
                for i, n in enumerate(("gsb", "usb", "g2", "tt", "sg", "gu"))}
        gs_bf = [carve(AR1, 32768 + 2048 * i, BF16, [P, 512]) for i in range(2)]
        x_tl2 = [carve(AR1, 36864 + 4096 * i, F32, [P, 512]) for i in range(2)]
        s_tl = [carve(SCR, 27648 + 2048 * i, BF16, [P, TH]) for i in range(2)]
        x_tl = carve(SCR, 31744, F32, [P, 512])
        tapv3 = tapv_t[:, :]

        sems = {n: es.enter_context(nc.semaphore("c_" + n))
                for n in ("sy", "gp", "pe", "dv", "ac", "ad")}
        block = es.enter_context(nc.Block())

        def program(em):
            V = em.proxy("vector")
            A = em.proxy("scalar")
            Tn = em.proxy("tensor")
            Sy = em.proxy("sync")
            G = em.proxy("gpsimd")

            def tap2d(name, src_view, dst_slice_fn, n0, n1, w):
                if name not in tap_t:
                    return
                for i in range(n0):
                    for j in range(n1):
                        V.wait("pe")
                        V.wait("sy")
                        em.inc(V.tensor_copy(tapv[:, 0:w], src_view(i, j)), "dv")
                        Sy.wait("dv")
                        em.inc(Sy.dma_start(out=dst_slice_fn(i, j), in_=tapv[:, 0:w]), "sy", 16)
                        Sy.wait("sy")

            for dst, src in ((ident_bf, ident_in), (trim_bf, trim_in),
                             (ones_bf, ones_in)):
                em.inc(G.dma_start(out=dst[:], in_=src[:]), "gp", 16)
            em.inc(G.dma_start(out=cos_sb[:].rearrange("p (a b) -> p a b", a=16),
                               in_=cosw.rearrange("(a p) c -> p a c", p=128)), "gp", 16)
            em.inc(G.dma_start(out=sin_sb[:].rearrange("p (a b) -> p a b", a=16),
                               in_=sinw.rearrange("(a p) c -> p a c", p=128)), "gp", 16)
            em.inc(Sy.dma_start(out=bout_sb[:], in_=bout[:]), "sy", 16)
            c_pre_gp = em.c["gp"]

            em.inc(V.memset(vaug[:, :, :].rearrange("p a (g c) -> p a g c", g=8)[:, :, :, 64:65], 1.0), "dv")

            Tn.wait("gp", c_pre_gp)
            c_p0mm = {}
            for tc in range(4):
                pp = tc % 2
                if tc >= 2:
                    Sy.wait("dv")
                h = Sy.dma_start(out=xchunk[pp][:],
                                 in_=xT.rearrange("(a p) t -> p a t", p=128)[:, :, 512 * tc:512 * (tc + 1)])
                em.inc(h, "sy", 16)
                c_ldx = em.c["sy"]
                for a in range(16):
                    pa = a % 2
                    V.wait("sy", c_ldx)
                    V.wait("pe", c_p0mm.get((tc, a - 2), c_p0mm.get((tc - 1, 14 + (a % 2)), 0)))
                    em.inc(V.tensor_copy(xbf[pa][:], xchunk[pp][:, a, :]), "dv")
                    A.wait("sy", c_ldx)
                    A.wait("pe", c_p0mm.get((tc, a - 2), c_p0mm.get((tc - 1, 14 + (a % 2)), 0)))
                    em.inc(A.activation(x2bf[pa][:], xchunk[pp][:, a, :], AF.Square), "ac")
                    Tn.wait("dv")
                    Tn.wait("ac")
                    em.inc(Tn.matmul(PS[0:1, 0:512], ones_bf[:, 0:1], xbf[pa][:],
                                     start=(a == 0), stop=(a == 15)), "pe")
                    em.inc(Tn.matmul(PS[0:1, 512:1024], ones_bf[:, 0:1], x2bf[pa][:],
                                     start=(a == 0), stop=(a == 15)), "pe")
                    c_p0mm[(tc, a)] = em.c["pe"]
                c_sums = em.c["pe"]
                V.wait("pe", c_sums)
                em.inc(V.tensor_scalar(rowST[:, 0:512], PS[0:1, 0:512], 1.0 / 2048.0, None, ALU.mult), "dv")
                em.inc(V.tensor_scalar(rowST[:, 512:1024], PS[0:1, 512:1024], 1.0 / 2048.0, None, ALU.mult), "dv")
                em.inc(V.tensor_tensor(rowAB[:, 0:512], rowST[:, 0:512], rowST[:, 0:512], ALU.mult), "dv")
                em.inc(V.scalar_tensor_tensor(rowAB[:, 0:512], rowAB[:, 0:512], -1.0,
                                              rowST[:, 512:1024], ALU.mult, ALU.add), "dv")
                em.inc(V.tensor_scalar(rowAB[:, 0:512], rowAB[:, 0:512], 1e-5, None, ALU.add), "dv")
                A.wait("dv")
                em.inc(A.activation(rowAB[:, 0:512], rowAB[:, 0:512], AF.Sqrt), "ac")
                V.wait("ac")
                em.inc(V.reciprocal(rowAB[:, 0:512], rowAB[:, 0:512]), "dv")
                em.inc(V.tensor_tensor(rowAB[:, 512:1024], rowST[:, 0:512], rowAB[:, 0:512], ALU.mult), "dv")
                em.inc(V.tensor_scalar(rowAB[:, 512:1024], rowAB[:, 512:1024], -1.0, None, ALU.mult), "dv")
                em.inc(V.tensor_copy(rowBF[:, 0:1024], rowAB[:, 0:1024]), "dv")
                Tn.wait("dv")
                em.inc(Tn.matmul(PS[:, 1024:1536], ones_bf[0:1, :], rowBF[:, 0:512], start=True, stop=True), "pe")
                em.inc(Tn.matmul(PS[:, 1536:2048], ones_bf[0:1, :], rowBF[:, 512:1024], start=True, stop=True), "pe")
                V.wait("pe")
                em.inc(V.tensor_copy(ab[:], PS[:, 1024:2048]), "dv")
                for a in range(16):
                    em.inc(V.tensor_tensor(lt1[:], xchunk[pp][:, a, :], ab[:, 0:512], ALU.mult), "dv")
                    em.inc(V.tensor_tensor(xn[:, a, 512 * tc:512 * (tc + 1)], lt1[:], ab[:, 512:1024], ALU.add), "dv")
                c_xn_tc = em.c["dv"]
                Sy.wait("dv", c_xn_tc)
                h = Sy.dma_start(out=xn_hbm.rearrange("(a p) t -> p a t", p=128)[:, :, 512 * tc:512 * (tc + 1)],
                                 in_=xn[:, :, 512 * tc:512 * (tc + 1)])
                em.inc(h, "sy", 16)
            c_p0_dv = em.c["dv"]
            c_xn_spill = em.c["sy"]

            tap2d("t_xn",
                  lambda a, c: xn[:, a, 512 * c:512 * (c + 1)],
                  lambda a, c: tap_t["t_xn"].rearrange("(a p) t -> p a t", p=128)[:, a, 512 * c:512 * (c + 1)],
                  16, 4, 512)

            if last_phase < 1:
                Sy.wait("sy", em.c["sy"])
                Sy.wait("gp", em.c["gp"])
                return
            em.inc(V.memset(qn2m[:], 0.0), "dv")
            em.inc(V.memset(kn2m[:], 0.0), "dv")
            V.wait("gp", c_pre_gp)

            qcol_w = [wq[:, 512 * i:512 * (i + 1)] for i in range(4)] + [wk, wv]
            c_evac_ac = {0: 0, 1: 0}
            for cc in range(6):
                G.wait("pe")
                G.wait("dv", c_p0_dv)
                h = G.dma_start(out=wchunk[:], in_=qcol_w[cc].rearrange("(a p) n -> p a n", p=128))
                em.inc(h, "gp", 16)
                c_wld = em.c["gp"]
                for tt in range(16):
                    reg = bank(tt % 2)
                    Tn.wait("gp", c_wld)
                    Tn.wait("ac", c_evac_ac[tt % 2])
                    for kt in range(16):
                        em.inc(Tn.matmul(reg, xn[:, kt, 128 * tt:128 * (tt + 1)], wchunk[:, kt, :],
                                         start=(kt == 0), stop=(kt == 15)), "pe")
                    c_mmt = em.c["pe"]
                    A.wait("pe", c_mmt)
                    if cc == 5:
                        em.inc(A.activation(vaug[:, tt, :].rearrange("p (g c) -> p g c", g=8)[:, :, 0:64],
                                            reg.rearrange("p (g c) -> p g c", g=8), AF.Copy), "ac")
                        c_evac_ac[tt % 2] = em.c["ac"]
                        continue
                    A.wait("dv")
                    A.wait("pe")
                    dst = qsc if cc < 4 else krop
                    em.inc(A.activation(dst[:], reg, AF.Copy, scale=(0.125 if cc < 4 else 1.0)), "ac")
                    c_evq = em.c["ac"]
                    c_evac_ac[tt % 2] = c_evq
                    V.wait("ac", c_evq)
                    src3 = dst[:].rearrange("p (g c) -> p g c", g=8)
                    cosv = cos_sb[:, 32 * tt:32 * (tt + 1)].unsqueeze(1).broadcast_to([P, 8, 32])
                    sinv = sin_sb[:, 32 * tt:32 * (tt + 1)].unsqueeze(1).broadcast_to([P, 8, 32])
                    x1 = src3[:, :, 0:32]
                    x2 = src3[:, :, 32:64]
                    m0, m1, m2, m3 = (r[:].rearrange("p (g c) -> p g c", g=8) for r in ropem)
                    em.inc(V.tensor_tensor(m0, x1, cosv, ALU.mult), "dv")
                    em.inc(V.tensor_tensor(m1, x2, sinv, ALU.mult), "dv")
                    em.inc(V.tensor_tensor(m2, x1, sinv, ALU.mult), "dv")
                    em.inc(V.tensor_tensor(m3, x2, cosv, ALU.mult), "dv")
                    if cc < 4:
                        out3 = qtok[:, tt, 512 * cc:512 * (cc + 1)].rearrange("p (g c) -> p g c", g=8)
                    else:
                        out3 = krop[:].rearrange("p (g c) -> p g c", g=8)
                    em.inc(V.tensor_tensor(out3[:, :, 0:32], m0, m1, ALU.subtract), "dv")
                    em.inc(V.tensor_tensor(out3[:, :, 32:64], m2, m3, ALU.add), "dv")
                    c_rope = em.c["dv"]
                    if cc == 4:
                        A.wait("dv", c_rope)
                        em.inc(A.activation(q2scr[:, 0:512], krop[:], AF.Square), "ac")
                        V.wait("ac")
                        for g in range(8):
                            em.inc(V.reduce_sum(kn2c[:, g:g + 1], q2scr[:, 64 * g:64 * (g + 1)], axis=AX.X), "dv")
                        em.inc(V.tensor_tensor(kn2m[:], kn2m[:], kn2c[:], ALU.max), "dv")
                        Tn.wait("dv", c_rope)
                        Tn.wait("dv")
                        tp6 = bank(6).bitcast(BF16)
                        for j in range(4):
                            em.inc(Tn.transpose(tp6[:, 128 * j:128 * (j + 1)],
                                                krop[:, 128 * j:128 * (j + 1)], ident_bf[:]), "pe")
                        c_ktr = em.c["pe"]
                        V.wait("pe", c_ktr)
                        for j in range(4):
                            em.inc(V.tensor_copy(kT[:, j, 128 * tt:128 * (tt + 1)],
                                                 tp6[:, 128 * j:128 * (j + 1)]), "dv")

            A.wait("dv")
            for tt in range(16):
                A.wait("dv")
                em.inc(A.activation(q2scr[:], qtok[:, tt, :], AF.Square), "ac")
                V.wait("ac")
                for b in range(32):
                    em.inc(V.reduce_sum(qn2c[:, b:b + 1], q2scr[:, 64 * b:64 * (b + 1)], axis=AX.X), "dv")
                em.inc(V.tensor_tensor(qn2m[:], qn2m[:], qn2c[:], ALU.max), "dv")
            em.inc(V.reduce_max(nb[:, 0:1], qn2m[:], axis=AX.X), "dv")
            em.inc(V.reduce_max(nb[:, 1:2], kn2m[:], axis=AX.X), "dv")
            nbpad = q2scr[:, 0:128]
            em.inc(V.memset(nbpad, 0.0), "dv")
            em.inc(V.tensor_copy(nbpad[:, 0:2], nb[:, 0:2]), "dv")
            Tn.wait("dv")
            tp7 = bank(7).bitcast(BF16)
            em.inc(Tn.transpose(tp7[:, 0:128], nbpad, ident_bf[:]), "pe")
            V.wait("pe")
            em.inc(V.reduce_max(nb[0:2, 2:3], tp7[0:2, 0:128], axis=AX.X), "dv")
            A.wait("dv")
            em.inc(A.activation(nb[0:2, 3:4], nb[0:2, 2:3], AF.Ln), "ac")
            V.wait("ac")
            em.inc(V.memset(ab[0:2, 0:128], 1.0), "dv")
            em.inc(V.tensor_scalar(ab[0:2, 128:256], ab[0:2, 0:128], nb[0:2, 3:4], None, ALU.mult), "dv")
            em.inc(V.tensor_copy(q2scr[0:2, 0:128], ab[0:2, 128:256]), "dv")
            Tn.wait("dv")
            em.inc(Tn.matmul(PS[:, 3584:3712], ones_bf[0:2, :], q2scr[0:2, 0:128], start=True, stop=True), "pe")
            A.wait("pe")
            em.inc(A.activation(nb[:, 6:7], PS[:, 3584:3585], AF.Exp, scale=0.5), "ac")
            V.wait("ac")
            em.inc(V.tensor_scalar(nb[:, 6:7], nb[:, 6:7], BCLAMP, None, ALU.min), "dv")
            em.inc(V.tensor_scalar(negB[:], nb[:, 6:7], -1.0, None, ALU.mult), "dv")
            if "t_nb" in tap_t:
                em.inc(V.tensor_copy(tapv[:, 0:8], nb[:]), "dv")
                em.inc(V.tensor_copy(tapv[:, 8:9], negB[:]), "dv")
                Sy.wait("dv")
                em.inc(Sy.dma_start(out=tap_t["t_nb"][:, 0:9], in_=tapv[:, 0:9]), "sy", 16)
                Sy.wait("sy")

            tap2d("t_q",
                  lambda tt, c: qtok[:, tt, 512 * c:512 * (c + 1)],
                  lambda tt, c: tap_t["t_q"].rearrange("(a p) t -> p a t", p=128)[:, tt, 512 * c:512 * (c + 1)],
                  16, 4, 512)
            tap2d("t_kT",
                  lambda j, c: kT[:, j, 512 * c:512 * (c + 1)],
                  lambda j, c: tap_t["t_kT"].rearrange("(a p) t -> p a t", p=128)[:, j, 512 * c:512 * (c + 1)],
                  4, 4, 512)

            if last_phase < 2:
                Sy.wait("sy", em.c["sy"])
                Sy.wait("gp", em.c["gp"])
                return
            c_exp_hist = {0: 0, 1: 0}
            c_bank03_dv = 0
            for ip, (hl, hh) in enumerate(PAIRS):
                a_tile = ip // 4
                tq01 = PS[:, 0:1024].bitcast(BF16)
                Tn.wait("ac", max(c_exp_hist[0], c_exp_hist[1]))
                Tn.wait("dv", c_bank03_dv if ip else 0)
                for tt in range(16):
                    em.inc(Tn.transpose(tq01[:, 128 * tt:128 * (tt + 1)],
                                        qtok[:, tt, 128 * ip:128 * (ip + 1)], ident_bf[:]), "pe")
                c_qtr = em.c["pe"]
                V.wait("pe", c_qtr)
                em.inc(V.tensor_copy(qT[:], tq01[:]), "dv")
                c_qT = em.c["dv"]
                for hi, h_ in enumerate((hl, hh)):
                    base = 64 * hi
                    g = h_ // 4
                    steps = []
                    for tkt in range(16):
                        for half in range(2):
                            lo = 128 * tkt + 1024 * half
                            hiw = min(128 * tkt + 1024 * (half + 1), T)
                            if lo < hiw:
                                steps.append((tkt, half, lo, hiw))
                    sdep = [None] * len(steps)
                    for i in range(len(steps) + 1):
                        if i < len(steps):
                            tkt, half, lo, hiw = steps[i]
                            w = hiw - lo
                            ppq = i % 2
                            Tn.wait("ac", c_exp_hist[ppq])
                            if i < 2:
                                Tn.wait("dv", max(c_qT, c_bank03_dv))
                            for j in range((w + 511) // 512):
                                wj = min(512, w - 512 * j)
                                em.inc(Tn.matmul(PS[:, 1024 * ppq + 512 * j:1024 * ppq + 512 * j + wj],
                                                 kT[base:base + 64, a_tile, 128 * tkt:128 * (tkt + 1)],
                                                 qT[base:base + 64, lo + 512 * j:lo + 512 * j + wj],
                                                 start=True, stop=True), "pe")
                            c_smm = em.c["pe"]
                            A.wait("pe", c_smm)
                            em.inc(A.activation(expb[ppq][:, 0:w], PS[:, 1024 * ppq:1024 * ppq + w],
                                                AF.Exp, bias=negB[:]), "ac")
                            c_exp_hist[ppq] = em.c["ac"]
                            if "t_exp" in tap_t and ip == 0 and hi == 0 and i == 0:
                                V.wait("ac")
                                em.inc(V.tensor_copy(tapv[:, 0:512], expb[ppq][:, 0:512]), "dv")
                                Sy.wait("dv")
                                em.inc(Sy.dma_start(out=tap_t["t_exp"][:, 0:512], in_=tapv[:, 0:512]), "sy", 16)
                                Sy.wait("sy")
                            if half == 0:
                                V.wait("ac", c_exp_hist[ppq])
                                V.wait("pe", c_smm)
                                em.inc(V.tensor_tensor(expb[ppq][:, 0:128], expb[ppq][:, 0:128],
                                                       trim_bf[:], ALU.mult), "dv")
                                sdep[i] = ("dv", em.c["dv"])
                            else:
                                sdep[i] = ("ac", c_exp_hist[ppq])
                        if i >= 1:
                            tkt, half, lo, hiw = steps[i - 1]
                            ppq = (i - 1) % 2
                            sem, cnt = sdep[i - 1]
                            Tn.wait(sem, cnt)
                            pos = lo
                            while pos < hiw:
                                cbank = pos // 512
                                cw = min(512 * (cbank + 1), hiw) - pos
                                em.inc(Tn.matmul(PS[0:65, 2048 + pos:2048 + pos + cw],
                                                 vaug[:, tkt, 65 * g:65 * g + 65],
                                                 expb[ppq][:, pos - lo:pos - lo + cw],
                                                 start=(tkt == 0), stop=(tkt == 4 * cbank + 3)), "pe")
                                pos += cw
                    c_av = em.c["pe"]
                    A.wait("pe", c_av)
                    em.inc(A.activation(norm32[64:65, :], PS[64:65, 2048:4096], AF.Ln), "ac")
                    em.inc(A.activation(normbf[64:65, :], norm32[64:65, :], AF.Exp, scale=-1.0), "ac")
                    c_rc = em.c["ac"]
                    Tn.wait("ac", c_rc)
                    for j in range(4):
                        em.inc(Tn.matmul(PS[0:64, 512 * j:512 * (j + 1)], ones_bf[64:65, 0:64],
                                         normbf[64:65, 512 * j:512 * (j + 1)], start=True, stop=True), "pe")
                    V.wait("pe")
                    em.inc(V.tensor_copy(recip[:], PS[0:64, 0:2048]), "dv")
                    c_bank03_dv = em.c["dv"]
                    if h_ % 2 == 0:
                        em.inc(V.tensor_tensor(ctxT[0:64, h_ // 2, :], PS[0:64, 2048:4096],
                                               recip[:], ALU.mult), "dv")
                    else:
                        V.wait("gp")
                        em.inc(V.tensor_tensor(oddt[:], PS[0:64, 2048:4096], recip[:], ALU.mult), "dv")
                        c_odd = em.c["dv"]
                        G.wait("dv", c_odd)
                        em.inc(G.dma_start(out=ctxT[64:128, h_ // 2, :], in_=oddt[:]), "gp", 16)
                    Tn.wait("dv")
            c_ctx_gp = em.c["gp"]

            tap2d("t_ctx",
                  lambda a, c: ctxT[:, a, 512 * c:512 * (c + 1)],
                  lambda a, c: tap_t["t_ctx"].rearrange("(a p) t -> p a t", p=128)[:, a, 512 * c:512 * (c + 1)],
                  16, 4, 512)

            c_sev_hist = {0: 0, 1: 0}
            c_wo_mms = {0: 0, 1: 0}
            c_wo_ld = {}
            for ph in range(17):
                if ph < 16:
                    ht = ph
                    G.wait("pe", c_wo_mms[ht % 2])
                    if ht == 0:
                        G.wait("pe")
                    em.inc(G.dma_start(out=woch[ht % 2][:],
                                       in_=wo[:, 128 * ht:128 * (ht + 1)].rearrange("(a p) n -> p a n", p=128)), "gp", 16)
                    c_wo_ld[ht] = em.c["gp"]
                if ph >= 1:
                    ht = ph - 1
                    for tc in range(4):
                        reg = bank(tc % 2)
                        Tn.wait("gp", c_wo_ld[ht])
                        if ht == 0 and tc == 0:
                            Tn.wait("gp", c_ctx_gp)
                            Tn.wait("dv")
                        Tn.wait("dv", c_sev_hist[tc % 2])
                        for kt in range(16):
                            em.inc(Tn.matmul(reg, woch[ht % 2][:, kt, :], ctxT[:, kt, 512 * tc:512 * (tc + 1)],
                                             start=(kt == 0), stop=(kt == 15)), "pe")
                        c_so = em.c["pe"]
                        c_wo_mms[ht % 2] = c_so
                        V.wait("pe", c_so)
                        V.wait("sy")
                        em.inc(V.tensor_scalar(stile[tc % 2][:], reg, bout_sb[:, ht:ht + 1], None, ALU.add), "dv")
                        c_sev = em.c["dv"]
                        c_sev_hist[tc % 2] = c_sev
                        Sy.wait("dv", c_sev)
                        em.inc(Sy.dma_start(out=sT_hbm[128 * ht:128 * (ht + 1), 512 * tc:512 * (tc + 1)],
                                            in_=stile[tc % 2][:]), "sy", 16)
            c_s_spill = em.c["sy"]

            if last_phase < 3:
                Sy.wait("sy", em.c["sy"])
                Sy.wait("gp", em.c["gp"])
                return
            P126 = 2.0 ** 126

            def dequant(pkb_flat, wtile, sxbuf):
                sx_flat = sxbuf[:].rearrange("p a n -> p (a n)")
                em.inc(V.tensor_scalar(dqt1[:], pkb_flat, 12, 0x8000, ALU.logical_shift_left, ALU.bitwise_and), "dv")
                em.inc(V.tensor_scalar(dqt2[:], pkb_flat, 6, 0x1C0, ALU.logical_shift_left, ALU.bitwise_and), "dv")
                em.inc(V.tensor_tensor(dqt1[:], dqt1[:], dqt2[:], ALU.bitwise_or), "dv")
                em.inc(V.scalar_tensor_tensor(wtile[:, 0:1024], dqt1[:].bitcast(BF16), P126, sx_flat, ALU.mult, ALU.mult), "dv")
                em.inc(V.tensor_scalar(dqt1[:], pkb_flat, 8, 0x8000, ALU.logical_shift_left, ALU.bitwise_and), "dv")
                em.inc(V.tensor_scalar(dqt2[:], pkb_flat, 2, 0x1C0, ALU.logical_shift_left, ALU.bitwise_and), "dv")
                em.inc(V.tensor_tensor(dqt1[:], dqt1[:], dqt2[:], ALU.bitwise_or), "dv")
                em.inc(V.scalar_tensor_tensor(wtile[:, 1024:2048], dqt1[:].bitcast(BF16), P126, sx_flat, ALU.mult, ALU.mult), "dv")
                return em.c["dv"]

            c_dq_hist = {}
            c_mms_hist = {}
            c_glue_dv = {0: 0, 1: 0}
            c_sq_ac = {0: 0, 1: 0}
            c_dqd_hist = {}
            c_dmm_hist = {}
            c_gacc_hist = {}
            def gate_load_dq(nsk2, ns2, sub):
                ksub = nsk2 * 16 + sub
                gsub = 16 * ns2 + sub
                pp = gsub % 2
                war = c_dq_hist.get(ksub - 2, 0)
                Sy.wait("dv", war)
                G.wait("dv", war)
                em.inc(Sy.dma_start(out=gpkb[pp][:],
                                    in_=gpk[1024 * gsub:1024 * (gsub + 1), :].rearrange("(a p) n -> p a n", p=128)), "sy", 16)
                em.inc(Sy.dma_start(out=upkb[pp][:],
                                    in_=upk[1024 * gsub:1024 * (gsub + 1), :].rearrange("(a p) n -> p a n", p=128)), "sy", 16)
                c_pk = em.c["sy"]
                em.inc(G.dma_start(out=sexp["g"][pp][:],
                                   in_=gsc[1024 * gsub:1024 * (gsub + 1), :].rearrange("(a p) n -> p a n", p=128)), "gp", 16)
                em.inc(G.dma_start(out=sexp["u"][pp][:],
                                   in_=usc[1024 * gsub:1024 * (gsub + 1), :].rearrange("(a p) n -> p a n", p=128)), "gp", 16)
                c_gpk = em.c["gp"]
                V.wait("pe", c_mms_hist.get(ksub - 2, 0))
                V.wait("sy", c_pk)
                V.wait("gp", c_gpk)
                dequant(gpkb[pp][:].rearrange("p a n -> p (a n)"),
                        wgb[pp][:].rearrange("p a n -> p (a n)"), sexp["g"][pp])
                c_dq_hist[ksub] = dequant(upkb[pp][:].rearrange("p a n -> p (a n)"),
                                          wub[pp][:].rearrange("p a n -> p (a n)"), sexp["u"][pp])

            def down_load_dq(nsk2, ns2, hc):
                khc = nsk2 * 16 + hc
                ppd = hc % 2
                war = c_dqd_hist.get(khc - 2, 0)
                Sy.wait("dv", war)
                G.wait("dv", war)
                em.inc(Sy.dma_start(out=dpkb[ppd][:],
                                    in_=dpk[8192 * hc + 1024 * ns2:8192 * hc + 1024 * (ns2 + 1), :].rearrange("(a p) n -> p a n", p=128)), "sy", 16)
                c_dl = em.c["sy"]
                em.inc(G.dma_start(out=sexp["d"][ppd][:],
                                   in_=dsc[8192 * hc + 1024 * ns2:8192 * hc + 1024 * (ns2 + 1), :].rearrange("(a p) n -> p a n", p=128)), "gp", 16)
                c_gl = em.c["gp"]
                V.wait("pe", c_dmm_hist.get(khc - 2, 0))
                V.wait("sy", c_dl)
                V.wait("gp", c_gl)
                c_dqd_hist[khc] = dequant(dpkb[ppd][:].rearrange("p a n -> p (a n)"),
                                          wdb[ppd][:].rearrange("p a n -> p (a n)"), sexp["d"][ppd])

            for th in range(2):
                t0 = TH * th
                Sy.wait("sy", c_xn_spill)
                if th == 1:
                    Sy.wait("pe")
                em.inc(Sy.dma_start(out=xnh[:],
                                    in_=xn_hbm.rearrange("(a p) t -> p a t", p=128)[:, :, t0:t0 + TH]), "sy", 16)
                c_xnh = em.c["sy"]
                for ns in range(8):
                    nsk = th * 8 + ns
                    first_block = (th == 0 and ns == 0)
                    for ph in range(17):
                        if ph < 16 and (ph >= 2 or first_block):
                            gate_load_dq(nsk, ns, ph)
                        if ph == 15:
                            down_load_dq(nsk, ns, 0)
                        if ph == 16:
                            down_load_dq(nsk, ns, 1)
                        if ph >= 1:
                            sub = ph - 1
                            ksub = nsk * 16 + sub
                            pp = (16 * ns + sub) % 2
                            for qt in range(2):
                                q0 = 512 * qt
                                gb = bank(qt)
                                ub = bank(2 + qt)
                                Tn.wait("dv", max(c_dq_hist[ksub], c_glue_dv[qt]))
                                Tn.wait("sy", c_xnh)
                                Tn.wait("ac", c_sq_ac[qt])
                                for kt in range(16):
                                    em.inc(Tn.matmul(gb, wgb[pp][:, kt, :], xnh[:, kt, q0:q0 + 512],
                                                     start=(kt == 0), stop=(kt == 15)), "pe")
                                for kt in range(16):
                                    em.inc(Tn.matmul(ub, wub[pp][:, kt, :], xnh[:, kt, q0:q0 + 512],
                                                     start=(kt == 0), stop=(kt == 15)), "pe")
                                c_gu = em.c["pe"]
                                A.wait("pe", c_gu)
                                A.wait("dv", max(c_glue_dv[0], c_glue_dv[1]))
                                em.inc(A.activation(glue["gsb"][:], gb, AF.Copy), "ac")
                                em.inc(A.activation(glue["usb"][:], ub, AF.Copy), "ac")
                                em.inc(A.activation(glue["g2"][:], gb, AF.Square), "ac")
                                c_sq_ac[qt] = em.c["ac"]
                                V.wait("ac")
                                em.inc(V.tensor_scalar(glue["g2"][:], glue["g2"][:], GELU_C, 1.0,
                                                       ALU.mult, ALU.add), "dv")
                                em.inc(V.tensor_tensor(glue["tt"][:], glue["g2"][:], glue["gsb"][:], ALU.mult), "dv")
                                em.inc(V.tensor_tensor(glue["gu"][:], glue["gsb"][:], glue["usb"][:], ALU.mult), "dv")
                                A.wait("dv")
                                em.inc(A.activation(glue["sg"][:], glue["tt"][:], AF.Sigmoid, scale=2.0 * GELU_S), "ac")
                                V.wait("ac")
                                em.inc(V.tensor_tensor(inter[:, sub, q0:q0 + 512], glue["sg"][:], glue["gu"][:], ALU.mult), "dv")
                                c_glue_dv[qt] = em.c["dv"]
                            c_mms_hist[ksub] = c_gu
                    for ph in range(17):
                        if 2 <= ph < 16:
                            down_load_dq(nsk, ns, ph)
                        if ph == 15 and not (th == 1 and ns == 7):
                            ns2, nsk2 = (ns + 1, nsk + 1) if ns < 7 else (0, nsk + 1)
                            gate_load_dq(nsk2, ns2, 0)
                        if ph == 16 and not (th == 1 and ns == 7):
                            ns2, nsk2 = (ns + 1, nsk + 1) if ns < 7 else (0, nsk + 1)
                            gate_load_dq(nsk2, ns2, 1)
                        if ph >= 1:
                            hc = ph - 1
                            khc = nsk * 16 + hc
                            ppd = hc % 2
                            for qt2 in range(2):
                                q0 = 512 * qt2
                                db = bank(4 + 2 * ppd + qt2)
                                Tn.wait("dv", max(c_dqd_hist[khc], c_gacc_hist.get((khc - 2, qt2), 0)))
                                Tn.wait("sy", c_xnh)
                                for kt in range(16):
                                    em.inc(Tn.matmul(db, wdb[ppd][:, kt, :], inter[:, kt, q0:q0 + 512],
                                                     start=(kt == 0), stop=(kt == 15)), "pe")
                                c_dmm = em.c["pe"]
                                V.wait("pe", c_dmm)
                                if ns == 0:
                                    em.inc(V.tensor_copy(g_acc[:, hc, q0:q0 + 512], db), "dv")
                                else:
                                    em.inc(V.tensor_tensor(g_acc[:, hc, q0:q0 + 512],
                                                           g_acc[:, hc, q0:q0 + 512], db, ALU.add), "dv")
                                c_gacc_hist[(khc, qt2)] = em.c["dv"]
                            c_dmm_hist[khc] = c_dmm
                c_gacc = em.c["dv"]
                c_comb_hist = {}
                c_gsb_hist = {}
                c_xld_hist = {}
                c_sld_hist = {}
                for i in range(32):
                    hc, hh = i // 2, i % 2
                    pb = i % 2
                    if hh == 0:
                        A.wait("sy", c_s_spill)
                        A.wait("dv", c_comb_hist.get((hc - 2, 1), 0))
                        em.inc(A.dma_start(out=s_tl[hc % 2][:],
                                           in_=sT_hbm[128 * hc:128 * (hc + 1), t0:t0 + TH]), "ad", 16)
                        c_sld_hist[hc] = em.c["ad"]
                    tq0 = t0 + 512 * hh
                    xin = xT.rearrange("(h r) t -> r h t", h=2)[64 * hc:64 * (hc + 1), :, tq0:tq0 + 512]
                    if i < 2:
                        A.wait("pe")
                    A.wait("dv", c_comb_hist.get((hc - 1, hh), 0))
                    em.inc(A.dma_start(out=x_tl2[pb][:], in_=xin), "ad", 16)
                    c_xld_hist[i] = em.c["ad"]
                    A.wait("dv", max(c_gacc, c_gsb_hist.get(i - 2, 0)))
                    em.inc(A.activation(gs_bf[pb][:], g_acc[:, hc, 512 * hh:512 * (hh + 1)], AF.Silu), "ac")
                    c_gs = em.c["ac"]
                    V.wait("ac", c_gs)
                    V.wait("ad", max(c_xld_hist[i], c_sld_hist[hc]))
                    em.inc(V.tensor_tensor(gs_bf[pb][:], gs_bf[pb][:], s_tl[hc % 2][:, 512 * hh:512 * (hh + 1)], ALU.mult), "dv")
                    c_gsb_hist[i] = em.c["dv"]
                    em.inc(V.tensor_tensor(x_tl2[pb][:], x_tl2[pb][:], gs_bf[pb][:], ALU.add), "dv")
                    c_comb_hist[(hc, hh)] = em.c["dv"]
                    A.wait("dv", c_comb_hist[(hc, hh)])
                    em.inc(A.dma_start(out=outT[128 * hc:128 * (hc + 1), tq0:tq0 + 512], in_=x_tl2[pb][:]), "ad", 16)

                if "t_g" in tap_t:
                    for hc in range(16):
                        for c2 in range(2):
                            V.wait("sy")
                            em.inc(V.tensor_copy(tapv3[:], g_acc[:, hc, 512 * c2:512 * (c2 + 1)]), "dv")
                            Sy.wait("dv")
                            em.inc(Sy.dma_start(out=tap_t["t_g"][128 * hc:128 * (hc + 1),
                                                               t0 + 512 * c2:t0 + 512 * (c2 + 1)],
                                                in_=tapv3[:]), "sy", 16)
                            Sy.wait("sy")

            Sy.wait("sy", em.c["sy"])
            Sy.wait("gp", em.c["gp"])
            A.wait("ad", em.c["ad"])

        def mk(which):
            def cb(eng):
                program(Em(nc, which, eng, sems))
            return cb

        block.sync(mk("sync"))
        block.gpsimd(mk("gpsimd"))
        block.tensor(mk("tensor"))
        block.vector(mk("vector"))
        block.scalar(mk("scalar"))

    return nc


def host_inputs(Wq, Wk, Wv, Wo, b_out, gWp, gSc, uWp, uSc, dWp, dSc):
    pk = _perm_k()
    pn = _perm_n()
    qc = _qcol_order()

    def pack_blocks(Wp, perm_rows):
        Wt = np.ascontiguousarray(Wp.astype(np.uint16)[perm_rows, :].T)
        K2, N = Wt.shape
        nb = N // P
        return np.ascontiguousarray(Wt.reshape(K2, nb, P).transpose(1, 0, 2)).reshape(nb * K2, P)

    def pack_scales(Sc, perm_rows):
        St = np.ascontiguousarray(Sc[perm_rows, :].T.astype(np.float32))
        B, N = St.shape
        nb = N // P
        blocks = np.ascontiguousarray(St.reshape(B, nb, P).transpose(1, 0, 2))
        return np.ascontiguousarray(np.repeat(blocks, 16, axis=1)).reshape(nb * B * 16, P)

    inv = (1.0 / (10000.0 ** (np.arange(0, HD, 2, dtype=np.float32) / np.float32(HD)))).astype(np.float32)
    t = np.arange(T, dtype=np.float32)
    fr = np.outer(t, inv).astype(np.float32)

    return {
        "wq": np.ascontiguousarray(Wq[pk][:, qc]).astype(np.float32),
        "wk": np.ascontiguousarray(Wk[pk]).astype(np.float32),
        "wv": np.ascontiguousarray(Wv[pk]).astype(np.float32),
        "wo": np.ascontiguousarray(Wo).astype(np.float32),
        "bout": np.ascontiguousarray(np.asarray(b_out).reshape(16, P).T).astype(np.float32),
        "gpk": pack_blocks(gWp, pn),
        "upk": pack_blocks(uWp, pn),
        "dpk": pack_blocks(dWp, np.arange(D)),
        "gsc": pack_scales(gSc, pn),
        "usc": pack_scales(uSc, pn),
        "dsc": pack_scales(dSc, np.arange(D)),
        "cosw": np.cos(fr).astype(np.float32),
        "sinw": np.sin(fr).astype(np.float32),
        "ident_in": np.eye(P, dtype=np.float32),
        "trim_in": np.triu(np.ones((P, P), dtype=np.float32)),
        "ones_in": np.ones((P, P), dtype=np.float32),
    }


def kernel(L0_attn_x, L0_attn_Wq, L0_attn_Wk, L0_attn_Wv, L0_attn_Wo, L0_attn_b_out,
           L0_ffn_gate_W_packed, L0_ffn_gate_scales, L0_ffn_up_W_packed, L0_ffn_up_scales,
           L0_ffn_down_W_packed, L0_ffn_down_scales):
    x = np.asarray(L0_attn_x)
    shared = host_inputs(np.asarray(L0_attn_Wq), np.asarray(L0_attn_Wk),
                         np.asarray(L0_attn_Wv), np.asarray(L0_attn_Wo),
                         np.asarray(L0_attn_b_out),
                         np.asarray(L0_ffn_gate_W_packed), np.asarray(L0_ffn_gate_scales),
                         np.asarray(L0_ffn_up_W_packed), np.asarray(L0_ffn_up_scales),
                         np.asarray(L0_ffn_down_W_packed), np.asarray(L0_ffn_down_scales))
    pk = _perm_k()
    nc = build_program()
    in_maps = []
    for b in range(8):
        m = dict(shared)
        m["xT"] = np.ascontiguousarray(x[b].T[pk, :]).astype(np.float32)
        in_maps.append(m)
    res = run_bass_kernel_spmd(nc, in_maps, core_ids=list(range(8)))
    out = np.empty((8, T, D), dtype=np.float32)
    for b in range(8):
        out[b] = res.results[b]["outT"].T
    return out
